# version 26
# baseline (speedup 1.0000x reference)
import numpy as np
from contextlib import ExitStack

import concourse.bass as bass
import concourse.mybir as mybir
from concourse.bass_utils import run_bass_kernel_spmd

F32 = mybir.dt.float32
BF16 = mybir.dt.bfloat16
AF = mybir.ActivationFunctionType
ALU = mybir.AluOpType
AX = mybir.AxisListType

P = 128
D = 1024
H = 16
HD = 64
NC = 8
QB = 128.0
FEPS = 1e-5
LNE = 1e-5
LN_EPS = 1e-3
CLIP_HI = float(np.float32(np.float64(QB) - 1e-5))
CLIP_LO = float(np.float32(-np.float64(QB) + 1e-5))


def build_nc(nt):
    nc = bass.Bass("TRN2", target_bir_lowering=False,
                   detect_race_conditions=False)
    T = nt * P

    x_h = nc.dram_tensor("x", [T, D], F32, kind="ExternalInput").ap()
    wqT_h = nc.dram_tensor("wqT", [D, D], F32, kind="ExternalInput").ap()
    wkT_h = nc.dram_tensor("wkT", [D, D], F32, kind="ExternalInput").ap()
    wvT_h = nc.dram_tensor("wvT", [D, D], F32, kind="ExternalInput").ap()
    wf_h = nc.dram_tensor("wf", [D, D], F32, kind="ExternalInput").ap()
    lng_h = nc.dram_tensor("lng", [1, D], F32, kind="ExternalInput").ap()
    lnb_h = nc.dram_tensor("lnb", [1, D], F32, kind="ExternalInput").ap()
    ident_h = nc.dram_tensor("ident", [P, P], F32, kind="ExternalInput").ap()
    out_h = nc.dram_tensor("out", [T, D], F32, kind="ExternalOutput").ap()
    cc_in = nc.dram_tensor("cc_in", [1, 8], F32)
    cc_out = nc.dram_tensor("cc_out", [1, 8], F32)
    dbg_h = nc.dram_tensor("dbg", [1, 16], F32, kind="ExternalOutput").ap()

    sched = []
    cnt = {"t": 0, "v": 0, "a": 0, "dl": 0, "ds": 0, "cc": 0}

    def step(eng, emit, waits=(), inc=()):
        ws = [(k, val) for (k, val) in waits if val and val > 0]
        sched.append((eng, ws, emit, list(inc)))
        for sk, amt in inc:
            cnt[sk] += amt
        return cnt[inc[0][0]] if inc else None

    with ExitStack() as ctx:
        sbuf = lambda name, shape, dt=F32: ctx.enter_context(
            nc.sbuf_tensor(name, shape, dt))
        psum = lambda name, shape: ctx.enter_context(
            nc.psum_tensor(name, shape, F32))

        wqh_s = sbuf("wqh_s", [P, NC * D], BF16)
        wql_s = sbuf("wql_s", [P, NC * D], BF16)
        wkh_s = sbuf("wkh_s", [P, NC * D], BF16)
        wkl_s = sbuf("wkl_s", [P, NC * D], BF16)
        wvh_s = sbuf("wvh_s", [P, NC * D], BF16)
        wvl_s = sbuf("wvl_s", [P, NC * D], BF16)
        wf_s = sbuf("wf_s", [P, NC * D], BF16)
        gam_rep = sbuf("gam_rep", [P, D])
        bet_rep = sbuf("bet_rep", [P, D])
        ids = sbuf("ids", [P, P])
        ones_r = sbuf("ones_r", [1, P])
        m3s = sbuf("m3s", [P, nt]); r3s = sbuf("r3s", [P, nt])
        m4s = sbuf("m4s", [P, nt]); r4s = sbuf("r4s", [P, nt])
        rrs = sbuf("rrs", [P, nt]); bb2s = sbuf("bb2s", [P, nt])
        gacc = sbuf("gacc", [P, nt])
        ls = sbuf("ls", [P, 24])
        maxq = sbuf("maxq", [P, H]); sumq = sbuf("sumq", [P, H])
        rqp = sbuf("rqp", [P, 2 * H])
        maxk = sbuf("maxk", [P, HD]); sumk = sbuf("sumk", [P, HD])
        srow = sbuf("srow", [1, P])
        s11 = sbuf("s11", [1, 8])
        wbr = sbuf("wbr", [P, 2])
        gsc = sbuf("gsc", [P, 2])
        fsc = sbuf("fsc", [P, 1])
        epsc = sbuf("epsc", [P, 2])

        x1ba = sbuf("x1ba", [P, D])
        x1bb = sbuf("x1bb", [P, D])
        x1bs = (x1ba, x1bb)
        xin = sbuf("xin", [P, D])
        q0 = sbuf("q0", [P, D])
        vtok = sbuf("vtok", [P, D])
        scr2 = sbuf("scr2", [P, D])
        tl1 = sbuf("tl1", [P, D])
        tl2a = sbuf("tl2a", [P, D])
        tl2b = sbuf("tl2b", [P, D])
        tl2s = (tl2a, tl2b)
        sqfm = sbuf("sqfm", [64, H * P])
        skfm = sbuf("skfm", [64, H * P])
        vfm = sbuf("vfm", [64, 2 * H * P])
        ablk = sbuf("ablk", [P, 2048])
        vq4 = sbuf("vq4", [P, 2048])
        s2sb = vq4
        xqps = ((vq4[:, 0:512].bitcast(BF16), vq4[:, 512:1024].bitcast(BF16)),
                (vq4[:, 1024:1536].bitcast(BF16),
                 vq4[:, 1536:2048].bitcast(BF16)))

        ps_big = psum("ps_big", [P, 2048])
        ps_2 = psum("ps_2", [P, D])
        ps_3 = psum("ps_3", [P, D])
        ps_d = None

        sems = {k: ctx.enter_context(nc.semaphore("sem_" + k))
                for k in ("t", "v", "a", "dl", "ds", "cc", "cb")}
        block = ctx.enter_context(nc.Block())

        def mk_dma(dst, src):
            def f(e):
                return e.dma_start(out=dst, in_=src)
            return f

        def ln_stats(src_ap, m_dst, r_dst, b_dst, eps_ap, waits, lsb, dump):
            def e_sq(a):
                return a.activation(dump, src_ap, AF.Square,
                                    accum_out=ls[:, lsb + 1:lsb + 2])
            a0 = step("a", e_sq, waits=waits, inc=[("a", 1)])

            def e_red(v):
                v.tensor_reduce(ls[:, lsb:lsb + 1], src_ap, AX.X, ALU.add)
                v.tensor_scalar(ls[:, lsb + 3:lsb + 4], ls[:, lsb + 1:lsb + 2],
                                1.0 / D, None, ALU.mult)
                v.tensor_scalar(m_dst, ls[:, lsb:lsb + 1], 1.0 / D, None,
                                ALU.mult)
                v.tensor_scalar(ls[:, lsb + 6:lsb + 7], ls[:, lsb:lsb + 1],
                                -1.0 / D, None, ALU.mult)
                return v.scalar_tensor_tensor(
                    ls[:, lsb + 4:lsb + 5], m_dst, m_dst[:, 0:1],
                    ls[:, lsb + 3:lsb + 4], ALU.mult, ALU.subtract)
            v1 = step("v", e_red, waits=waits + [("a", a0)], inc=[("v", 1)])

            def e_ln(a):
                return a.activation(ls[:, lsb + 5:lsb + 6],
                                    ls[:, lsb + 4:lsb + 5], AF.Ln,
                                    bias=eps_ap, scale=-1.0)
            a1 = step("a", e_ln, waits=[("v", v1)], inc=[("a", 1)])

            def e_exp(a):
                return a.activation(r_dst, ls[:, lsb + 5:lsb + 6], AF.Exp,
                                    scale=-0.5)
            a2 = step("a", e_exp, waits=[("a", a1)], inc=[("a", 1)])

            def e_b(a):
                return a.activation(b_dst, ls[:, lsb + 6:lsb + 7], AF.Identity,
                                    scale=r_dst[:, 0:1])
            return step("a", e_b, waits=[("a", a2)], inc=[("a", 1)])

        step("sync", mk_dma(ids[:, :], ident_h[:, :]), inc=[("dl", 16)])
        step("sync", mk_dma(q0[0:1, 0:D], lng_h[0:1, :]), inc=[("dl", 16)])
        step("sync", mk_dma(scr2[0:1, 0:D], lnb_h[0:1, :]), inc=[("dl", 16)])
        dl_consts = cnt["dl"]

        def e_memsets(v):
            v.memset(ones_r[:, :], 1.0)
            v.memset(ablk[:, :], 0.0)
            v.memset(vfm[0:64, :], 0.0)
            v.memset(epsc[:, 0:1], float(LNE))
            v.memset(epsc[:, 1:2], float(LN_EPS))
            return v.memset(s11[:, :], 0.0)
        v_memset = step("v", e_memsets, inc=[("v", 1)])

        def e_repgb(t):
            r = None
            for j in range(2):
                r = t.matmul(ps_2[:, j * 512:(j + 1) * 512], ones_r[0:1, :],
                             q0[0:1, j * 512:(j + 1) * 512],
                             start=True, stop=True)
            for j in range(2):
                r = t.matmul(ps_3[:, j * 512:(j + 1) * 512], ones_r[0:1, :],
                             scr2[0:1, j * 512:(j + 1) * 512],
                             start=True, stop=True)
            return r
        t_repgb = step("t", e_repgb, waits=[("v", v_memset), ("dl", dl_consts)],
                       inc=[("t", 1)])

        def e_repgb_ev(v):
            v.tensor_copy(gam_rep[:, :], ps_2[:, :])
            return v.tensor_copy(bet_rep[:, :], ps_3[:, :])
        v_gb = step("v", e_repgb_ev, waits=[("t", t_repgb)], inc=[("v", 1)])

        dl_wbase = cnt["dl"]
        widx = 0
        wsteps = []
        wbufs = (tl1, tl2a, tl2b, x1bb, q0, vtok, scr2, xin)
        NSTG = len(wbufs)
        for (whi, wlo, whbm) in ((wqh_s, wql_s, wqT_h), (wkh_s, wkl_s, wkT_h),
                                 (wvh_s, wvl_s, wvT_h)):
            for c in range(NC):
                buf = wbufs[widx % NSTG]
                gate = wsteps[widx - NSTG][1] if widx >= NSTG else v_gb
                step("sync", mk_dma(buf[:, :], whbm[c * P:(c + 1) * P, :]),
                     waits=[("v", gate)], inc=[("dl", 16)])
                wdl = dl_wbase + 16 * min(widx + NSTG, 24)

                def e_whi(a, buf=buf, whi=whi, c=c):
                    return a.copy(whi[:, c * D:(c + 1) * D], buf[:, :])
                a_hi = step("a", e_whi, waits=[("dl", wdl)], inc=[("a", 1)])

                def e_wlo(v, buf=buf, whi=whi, wlo=wlo, c=c):
                    return v.tensor_sub(wlo[:, c * D:(c + 1) * D], buf[:, :],
                                        whi[:, c * D:(c + 1) * D])
                v_lo_w = step("v", e_wlo, waits=[("a", a_hi)], inc=[("v", 1)])
                wsteps.append((a_hi, v_lo_w))
                widx += 1

        dl_base = cnt["dl"]
        for c in range(NC):
            step("sync", mk_dma(wbufs[c][:, :], wf_h[c * P:(c + 1) * P, :]),
                 waits=[("v", wsteps[16 + c][1])], inc=[("dl", 16)])
        wf_dl_all = cnt["dl"]

        def mk_abs(buf, c):
            def e_abs(v):
                return v.tensor_reduce(ls[:, c:c + 1], buf[:, :], AX.X,
                                       ALU.add, apply_absolute_value=True)
            return e_abs

        v_prev = None
        for c in range(NC):
            v_prev = step("v", mk_abs(wbufs[c], c),
                          waits=[("dl", wf_dl_all)], inc=[("v", 1)])

        def e_bsum(v):
            return v.tensor_reduce(maxq[:, 0:1], ls[:, 0:8], AX.X, ALU.add)
        v_bsum = step("v", e_bsum, waits=[("v", v_prev)], inc=[("v", 1)])

        def e_btr(t):
            return t.transpose(ps_3[0:1, 0:P], maxq[:, 0:1], ids[:, :])
        t_btr = step("t", e_btr, waits=[("v", v_bsum)], inc=[("t", 1)])

        def e_beta1(v):
            return v.tensor_copy(srow[0:1, :], ps_3[0:1, 0:P])
        v_b1 = step("v", e_beta1, waits=[("t", t_btr)], inc=[("v", 1)])

        def e_beta2(v):
            return v.tensor_reduce(s11[0:1, 2:3], srow[0:1, :], AX.X, ALU.add)
        v_b2 = step("v", e_beta2, waits=[("v", v_b1)], inc=[("v", 1)])

        def e_beta3(v):
            return v.tensor_scalar(s11[0:1, 0:1], s11[0:1, 2:3], 1.0 / (D * D),
                                   None, ALU.mult)
        v_b3 = step("v", e_beta3, waits=[("v", v_b2)], inc=[("v", 1)])

        def e_beta4(v):
            return v.tensor_scalar(s11[0:1, 1:2], s11[0:1, 0:1], float(FEPS),
                                   None, ALU.add)
        v_b4 = step("v", e_beta4, waits=[("v", v_b3)], inc=[("v", 1)])

        def e_beta5(v):
            return v.reciprocal(s11[0:1, 1:2], s11[0:1, 1:2])
        v_beta = step("v", e_beta5, waits=[("v", v_b4)], inc=[("v", 1)])

        def e_brep(t):
            return t.matmul(ps_3[0:P, 0:2], ones_r[0:1, :], s11[0:1, 0:2],
                            start=True, stop=True)
        t_brep = step("t", e_brep, waits=[("v", v_beta)], inc=[("t", 1)])

        def e_brep_ev(v):
            return v.tensor_copy(wbr[:, :], ps_3[0:P, 0:2])
        v_brep = step("v", e_brep_ev, waits=[("t", t_brep)], inc=[("v", 1)])

        v_last_tern = v_brep

        def mk_tern(buf, c):
            def e_tern(v):
                v.tensor_scalar(x1ba[:, :], buf[:, :], wbr[:, 1:2], 0.5,
                                ALU.mult, ALU.is_gt)
                v.tensor_scalar(vq4[:, 0:D], buf[:, :], wbr[:, 1:2], -0.5,
                                ALU.mult, ALU.is_lt)
                v.tensor_scalar(ls[:, 9:10], epsc[:, 0:1], 1.0, None, ALU.mult)
                v.tensor_sub(wf_s[:, c * D:(c + 1) * D], x1ba[:, :],
                             vq4[:, 0:D])
                return v.tensor_scalar(ls[:, 9:10], epsc[:, 0:1], 1.0, None,
                                       ALU.mult)
            return e_tern

        for c in range(NC):
            v_last_tern = step("v", mk_tern(wbufs[c], c),
                               waits=[("v", v_brep)] if c == 0 else [],
                               inc=[("v", 1)])
        v_setup_done = v_last_tern

        def ln_stats_thunks(src_ap, m_dst, r_dst, b_dst, eps_ap, waits0,
                            lsb, dump, out, with_b=True):
            def t_sq():
                def e_sq(a):
                    return a.activation(dump, src_ap, AF.Square,
                                        accum_out=ls[:, lsb + 1:lsb + 2])
                out["sq"] = step("a", e_sq, waits=waits0(), inc=[("a", 1)])

            def t_red():
                def e_red(v):
                    v.tensor_reduce(ls[:, lsb:lsb + 1], src_ap, AX.X, ALU.add)
                    v.tensor_scalar(ls[:, lsb + 3:lsb + 4],
                                    ls[:, lsb + 1:lsb + 2], 1.0 / D, None,
                                    ALU.mult)
                    v.tensor_scalar(m_dst, ls[:, lsb:lsb + 1], 1.0 / D, None,
                                    ALU.mult)
                    v.tensor_scalar(ls[:, lsb + 6:lsb + 7],
                                    ls[:, lsb:lsb + 1], -1.0 / D, None,
                                    ALU.mult)
                    return v.scalar_tensor_tensor(
                        ls[:, lsb + 4:lsb + 5], m_dst, m_dst[:, 0:1],
                        ls[:, lsb + 3:lsb + 4], ALU.mult, ALU.subtract)
                out["red"] = step("v", e_red,
                                  waits=waits0() + [("a", out["sq"])],
                                  inc=[("v", 1)])

            def t_ln():
                def e_ln(a):
                    return a.activation(ls[:, lsb + 5:lsb + 6],
                                        ls[:, lsb + 4:lsb + 5], AF.Ln,
                                        bias=eps_ap, scale=-1.0)
                out["ln"] = step("a", e_ln, waits=[("v", out["red"])],
                                 inc=[("a", 1)])

            def t_exp():
                def e_exp(a):
                    return a.activation(r_dst, ls[:, lsb + 5:lsb + 6], AF.Exp,
                                        scale=-0.5)
                out["exp"] = step("a", e_exp, waits=[("a", out["ln"])],
                                  inc=[("a", 1)])

            def t_b():
                def e_b(a):
                    return a.activation(b_dst, ls[:, lsb + 6:lsb + 7],
                                        AF.Identity, scale=r_dst[:, 0:1])
                out["b"] = step("a", e_b, waits=[("a", out["exp"])],
                                inc=[("a", 1)])
            return [t_sq, t_red, t_ln, t_exp] + ([t_b] if with_b else [])

        HC = {}

        def pump(it, n=2):
            for _ in range(n):
                th = next(it, None)
                if th is None:
                    return
                th()

        def emit_head(ti, tit):
            row0 = ti * P
            lsb = 12 * (ti % 2)
            eps_l = epsc[:, 0:1]
            fill = ls[:, lsb + 9:lsb + 10]
            x_t = x1bs[ti % 2]
            h2 = HC.get(ti - 2)
            h1 = HC.get(ti - 1)
            t2 = {}
            HC[ti] = t2

            if ti == 0:
                step("sync", mk_dma(x_t[:, :], x_h[row0:row0 + P, :]),
                     waits=[("v", v_setup_done)], inc=[("dl", 16)])
                w_x = cnt["dl"]
                a_ln1 = ln_stats(x_t[:, :], ls[:, lsb + 8:lsb + 9],
                                 ls[:, lsb + 7:lsb + 8],
                                 ls[:, lsb + 2:lsb + 3], eps_l,
                                 [("dl", w_x)], lsb, vtok[:, :])

                def e_xin(a, lsb=lsb, x_t=x_t):
                    return a.activation(scr2[:, :], x_t[:, :], AF.Identity,
                                        bias=ls[:, lsb + 2:lsb + 3],
                                        scale=ls[:, lsb + 7:lsb + 8])
                a_xin = step("a", e_xin, waits=[("a", a_ln1)], inc=[("a", 1)])
            else:
                a_xin = PH[ti]["xin"]
            pump(tit)

            def e_xt(t):
                r = None
                for c in range(NC):
                    r = t.transpose(ps_2[:, c * P:(c + 1) * P],
                                    scr2[:, c * P:(c + 1) * P], ids[:, :])
                return r
            t_xt = step("t", e_xt,
                        waits=[("a", a_xin)] + ([("v", h1["v_atok"])] if h1 else []),
                        inc=[("t", 1)])

            xinh = xin[:, 0:512].bitcast(BF16)
            xinl = xin[:, 512:1024].bitcast(BF16)

            def e_xt_hi(a):
                return a.copy(xinh, ps_2[:, :])
            a_xinh0 = step("a", e_xt_hi, waits=[("t", t_xt)], inc=[("a", 1)])

            def e_xt_hi2(a, lsb=lsb):
                a.activation(ls[:, lsb + 10:lsb + 11], epsc[:, 0:1],
                             AF.Identity)
                return a.activation(ls[:, lsb + 10:lsb + 11], epsc[:, 0:1],
                                    AF.Identity)
            a_xinh = step("a", e_xt_hi2, inc=[("a", 1)])

            def e_xt_lo(v):
                return v.tensor_sub(xinl, ps_2[:, :], xinh)
            v_xinfm = step("v", e_xt_lo, waits=[("a", a_xinh)], inc=[("v", 1)])
            pump(tit)

            def mk_proj(whi, wlo, ps):
                def f(t):
                    r = None
                    for n in range(2):
                        dst = ps[:, n * 512:(n + 1) * 512]
                        for c in range(NC):
                            for i, (xa, wa) in enumerate(
                                    ((xinh, whi), (xinh, wlo), (xinl, whi))):
                                r = t.matmul(
                                    dst,
                                    xa.rearrange("p (c q) -> p c q", c=NC)[:, c, :],
                                    wa[:, c * D + n * 512:c * D + (n + 1) * 512],
                                    start=(c == 0 and i == 0),
                                    stop=(c == NC - 1 and i == 2))
                    return r
                return f

            qps = ps_big[:, 1024:2048]

            def e_q_hi(t):
                r = None
                for n in range(2):
                    dst = qps[:, n * 512:(n + 1) * 512]
                    for c in range(NC):
                        r = t.matmul(
                            dst,
                            xinh.rearrange("p (c q) -> p c q", c=NC)[:, c, :],
                            wqh_s[:, c * D + n * 512:c * D + (n + 1) * 512],
                            start=(c == 0), stop=False)
                return r
            step("t", e_q_hi, waits=[("a", a_xinh)], inc=[("t", 1)])

            def e_q_rest(t):
                r = None
                for n in range(2):
                    dst = qps[:, n * 512:(n + 1) * 512]
                    for c in range(NC):
                        for i, (xa, wa) in enumerate(
                                ((xinh, wql_s), (xinl, wqh_s))):
                            r = t.matmul(
                                dst,
                                xa.rearrange("p (c q) -> p c q", c=NC)[:, c, :],
                                wa[:, c * D + n * 512:c * D + (n + 1) * 512],
                                start=False,
                                stop=(c == NC - 1 and i == 1))
                return r
            t_q = step("t", e_q_rest, waits=[("v", v_xinfm)], inc=[("t", 1)])
            pump(tit)
            t_k = step("t", mk_proj(wkh_s, wkl_s, ps_3),
                       waits=[("v", h1["v_subk"])] if h1 else [],
                       inc=[("t", 1)])
            t_v = step("t", mk_proj(wvh_s, wvl_s, ps_big), waits=[],
                       inc=[("t", 1)])

            def e_maxq(v):
                return v.tensor_reduce(
                    maxq[:, :], qps[:, :].rearrange("p (h d) -> p h d", h=H),
                    AX.X, ALU.max)
            v_maxq = step("v", e_maxq, waits=[("t", t_q)], inc=[("v", 1)])

            def e_subq(v):
                return v.tensor_tensor(
                    q0[:, :].rearrange("p (h d) -> p h d", h=H),
                    qps[:, :].rearrange("p (h d) -> p h d", h=H),
                    maxq[:, :].unsqueeze(2).broadcast_to([P, H, HD]),
                    ALU.subtract)
            v_subq = step("v", e_subq,
                          waits=[("v", v_maxq)] +
                                ([("t", h1["t_hq2"])] if h1 else []),
                          inc=[("v", 1)])

            def e_expq(a):
                return a.activation(q0[:, :], q0[:, :], AF.Exp)
            a_expq0 = step("a", e_expq, waits=[("v", v_subq)], inc=[("a", 1)])

            def e_expq2(a, lsb=lsb):
                a.activation(ls[:, lsb + 11:lsb + 12], epsc[:, 0:1],
                             AF.Identity)
                return a.activation(ls[:, lsb + 11:lsb + 12], epsc[:, 0:1],
                                    AF.Identity)
            a_expq = step("a", e_expq2, inc=[("a", 1)])

            def e_sumq(v, fill=fill, ti=ti):
                v.tensor_reduce(sumq[:, :],
                                q0[:, :].rearrange("p (h d) -> p h d", h=H),
                                AX.X, ALU.add)
                v.tensor_scalar(fill, epsc[:, 0:1], 1.0, None, ALU.mult)
                return v.reciprocal(rqp[:, (ti % 2) * H:(ti % 2) * H + H],
                                    sumq[:, :])
            v_sumq = step("v", e_sumq, waits=[("a", a_expq)], inc=[("v", 1)])

            def e_maxk(v):
                return v.tensor_reduce(
                    maxk[:, :], ps_3[:, :].rearrange("p (h d) -> p d h", h=H),
                    AX.X, ALU.max)
            v_maxk = step("v", e_maxk, waits=[("t", t_k)], inc=[("v", 1)])

            def e_subk(v):
                return v.tensor_tensor(
                    scr2[:, :].rearrange("p (h d) -> p h d", h=H),
                    ps_3[:, :].rearrange("p (h d) -> p h d", h=H),
                    maxk[:, :].unsqueeze(1).broadcast_to([P, H, HD]),
                    ALU.subtract)
            v_subk = step("v", e_subk,
                          waits=[("v", v_maxk), ("t", t_xt)] +
                                ([("t", h1["t_hk2"])] if h1 else []),
                          inc=[("v", 1)])
            t2["v_subk"] = v_subk

            def e_expk(a):
                return a.activation(scr2[:, :], scr2[:, :], AF.Exp)
            a_expk = step("a", e_expk, waits=[("v", v_subk)], inc=[("a", 1)])

            def e_sumk(v, fill=fill):
                v.tensor_reduce(sumk[:, :],
                                scr2[:, :].rearrange("p (h d) -> p d h", h=H),
                                AX.X, ALU.add)
                v.tensor_scalar(fill, epsc[:, 0:1], 1.0, None, ALU.mult)
                return v.reciprocal(maxk[:, :], sumk[:, :])
            v_sumk = step("v", e_sumk, waits=[("a", a_expk)], inc=[("v", 1)])

            def e_mulk(v):
                return v.tensor_tensor(
                    scr2[:, :].rearrange("p (h d) -> p h d", h=H),
                    scr2[:, :].rearrange("p (h d) -> p h d", h=H),
                    maxk[:, :].unsqueeze(1).broadcast_to([P, H, HD]),
                    ALU.mult)
            v_sk = step("v", e_mulk, waits=[("v", v_sumk)], inc=[("v", 1)])
            t2["v_sk"] = v_sk

            def e_vev(v):
                return v.tensor_copy(vtok[:, :], ps_big[:, 0:D])
            v_vtok = step("v", e_vev, waits=[("t", t_v)], inc=[("v", 1)])
            pump(tit)

            psA = ps_big[:, 0:1024]
            psB = ps_big[:, 1024:2048]

            def mk_headT_half(src_b, dst, hs):
                def f(t):
                    r = None
                    for j, h in enumerate(hs):
                        r = t.transpose(dst[0:64, j * P:(j + 1) * P],
                                        src_b[:, h * HD:(h + 1) * HD],
                                        ids[:, :])
                    return r
                return f

            t_hv1 = step("t", mk_headT_half(vtok, psA, range(0, 8)),
                         waits=[("v", v_vtok)], inc=[("t", 1)])
            t_hv2 = step("t", mk_headT_half(vtok, psB, range(8, 16)),
                         waits=[("v", v_subq)], inc=[("t", 1)])
            t2["t_hv"] = t_hv2

            def mk_vfm_ev(src_ps, h0):
                def f(v):
                    s_ap = src_ps[0:64, 0:8 * P].rearrange(
                        "p (h t2) -> p h t2", h=8)
                    d_ap = vfm[0:64, :].rearrange(
                        "p (t2 u) -> p u t2", u=32)[:, h0:h0 + 8, :]
                    return v.tensor_copy(d_ap, s_ap)
                return f
            v_hv1 = step("v", mk_vfm_ev(psA, 0), waits=[("t", t_hv1)],
                         inc=[("v", 1)])
            v_hv2 = step("v", mk_vfm_ev(psB, 8), waits=[("t", t_hv2)],
                         inc=[("v", 1)])
            pump(tit)

            pump(tit)

            t_hq1 = step("t", mk_headT_half(q0, psA, range(0, 8)),
                         waits=[("a", a_expq), ("v", v_hv1)], inc=[("t", 1)])
            t_hq2 = step("t", mk_headT_half(q0, psB, range(8, 16)),
                         waits=[("v", v_hv2)], inc=[("t", 1)])
            t2["t_hq2"] = t_hq2

            def mk_sb_ev(src_ps, dst_sb, off, isv):
                def f(e):
                    s_ap = src_ps[0:64, 0:8 * P]
                    d_ap = dst_sb[0:64, off:off + 8 * P]
                    return (e.tensor_copy(d_ap, s_ap) if isv
                            else e.copy(d_ap, s_ap))
                return f
            v_hq1 = step("v", mk_sb_ev(psA, sqfm, 0, True),
                         waits=[("t", t_hq1)], inc=[("v", 1)])
            v_hq2 = step("v", mk_sb_ev(psB, sqfm, 8 * P, True),
                         waits=[("t", t_hq2)], inc=[("v", 1)])
            pump(tit)

            t_hk1 = step("t", mk_headT_half(scr2, psA, range(0, 8)),
                         waits=[("v", v_sk), ("v", v_hq1)], inc=[("t", 1)])
            t_hk2 = step("t", mk_headT_half(scr2, psB, range(8, 16)),
                         waits=[("v", v_hq2)], inc=[("t", 1)])
            a_hk1 = step("a", mk_sb_ev(psA, skfm, 0, False),
                         waits=[("t", t_hk1)], inc=[("a", 1)])
            a_hk2 = step("a", mk_sb_ev(psB, skfm, 8 * P, False),
                         waits=[("t", t_hk2)], inc=[("a", 1)])
            t2["t_hk2"] = t_hk2
            pump(tit)

            def mk_qt(dst, f0, f1):
                def f(t):
                    r = None
                    for f_ in range(f0, f1):
                        r = t.transpose(dst[0:P, (f_ - f0) * 64:(f_ - f0 + 1) * 64],
                                        vfm[0:64, f_ * P:(f_ + 1) * P],
                                        ids[0:64, 0:64])
                    return r
                return f
            t_qt1 = step("t", mk_qt(psA, 0, 16),
                         waits=[("a", a_hk1), ("v", v_hv2)], inc=[("t", 1)])

            sq_v = sqfm[0:64, :].rearrange("p (h t2) -> p t2 h", t2=P)
            sk_v = skfm[0:64, :].rearrange("p (h t2) -> p t2 h", t2=P)

            def e_s1(t):
                r = None
                for tok in range(P):
                    c = tok % 4
                    f = tok // 4
                    r = t.matmul(ps_2[32 * c:32 * c + H, 16 * f:16 * f + H],
                                 sk_v[:, tok, :], sq_v[:, tok, :],
                                 start=True, stop=True,
                                 tile_position=(0, 32 * c))
                return r
            t_s1 = step("t", e_s1, waits=[("v", v_hq2), ("a", a_hk2)],
                        inc=[("t", 1)])
            t_qt2 = step("t", mk_qt(psB, 16, 32), waits=[], inc=[("t", 1)])
            pump(tit)

            v_vq41 = step("v", mk_sb_ev(psA, vq4, 0, True)
                          if False else None, waits=[], inc=[]) if False else None
            def mk_vq4_ev(src_ps, off, isv):
                def f(e):
                    s_ap = src_ps[0:P, 0:1024]
                    d_ap = vq4[:, off:off + 1024]
                    return (e.tensor_copy(d_ap, s_ap) if isv
                            else e.copy(d_ap, s_ap))
                return f
            v_vq41 = step("v", mk_vq4_ev(psA, 0, True), waits=[("t", t_qt1)],
                          inc=[("v", 1)])
            a_vq42 = step("a", mk_vq4_ev(psB, 1024, False),
                          waits=[("t", t_qt2)], inc=[("a", 1)])

            def mk_aev(cs, isv):
                def f(e):
                    r = None
                    for c in cs:
                        s_ap = ps_2[32 * c:32 * c + H, 0:512].rearrange(
                            "p (f h) -> p f h", h=H)
                        d_ap = ablk[32 * c:32 * c + H, :].rearrange(
                            "p (f q) -> p f q", q=64)[:, :, 16 * c:16 * c + H]
                        r = e.tensor_copy(d_ap, s_ap) if isv else e.copy(d_ap, s_ap)
                    return r
                return f
            v_aev = step("v", mk_aev((0, 1), True), waits=[("t", t_s1)],
                         inc=[("v", 1)])
            a_aev = step("a", mk_aev((2, 3), False), waits=[("t", t_s1)],
                         inc=[("a", 1)])
            pump(tit)

            def mk_s2(dst, f0, f1):
                def f(t):
                    r = None
                    for f_ in range(f0, f1):
                        r = t.matmul(dst[0:64, (f_ - f0) * 64:(f_ - f0 + 1) * 64],
                                     ablk[:, f_ * 64:(f_ + 1) * 64],
                                     vq4[:, f_ * 64:(f_ + 1) * 64],
                                     start=True, stop=True,
                                     tile_position=(0, 0))
                    return r
                return f
            t_s21 = step("t", mk_s2(psA, 0, 16),
                         waits=[("v", v_aev), ("a", a_aev), ("v", v_vq41)],
                         inc=[("t", 1)])
            t_s22 = step("t", mk_s2(psB, 16, 32), waits=[("a", a_vq42)],
                         inc=[("t", 1)])
            t2["t_s2"] = t_s22

            v_s21 = step("v", mk_sb_ev(psA, s2sb, 0, True),
                         waits=[("t", t_s21)], inc=[("v", 1)])
            v_s22 = step("v", mk_sb_ev(psB, s2sb, 1024, True),
                         waits=[("t", t_s22)], inc=[("v", 1)])
            pump(tit)

            def mk_unt(dst, off):
                def f(t):
                    r = None
                    for j in range(16):
                        r = t.transpose(dst[0:64, j * 64:(j + 1) * 64],
                                        s2sb[0:64, off + j * 64:off + (j + 1) * 64],
                                        ids[0:64, 0:64])
                    return r
                return f
            t_unt1 = step("t", mk_unt(psA, 0), waits=[("v", v_s21)],
                          inc=[("t", 1)])
            t_unt2 = step("t", mk_unt(psB, 1024), waits=[("v", v_s22)],
                          inc=[("t", 1)])
            a_fm1 = step("a", mk_sb_ev(psA, s2sb, 0, False),
                         waits=[("t", t_unt1)], inc=[("a", 1)])
            a_fm2 = step("a", mk_sb_ev(psB, s2sb, 1024, False),
                         waits=[("t", t_unt2)], inc=[("a", 1)])
            for th in tit:
                th()

            fmo_v = s2sb[0:64, :].rearrange("p (t2 sub) -> p t2 sub", sub=H)

            def e_hunt(t):
                r = None
                for h in range(H):
                    r = t.transpose(ps_2[:, h * HD:(h + 1) * HD],
                                    fmo_v[:, :, h], ids[0:64, 0:64])
                return r
            t_hunt = step("t", e_hunt,
                          waits=[("a", a_fm1), ("a", a_fm2)],
                          inc=[("t", 1)])
            t2["t_hunt"] = t_hunt

            def e_atok(v, ti=ti):
                return v.tensor_tensor(
                    tl2s[ti % 2][:, :].rearrange("p (h d) -> p h d", h=H),
                    ps_2[:, :].rearrange("p (h d) -> p h d", h=H),
                    rqp[:, (ti % 2) * H:(ti % 2) * H + H].unsqueeze(2)
                       .broadcast_to([P, H, HD]),
                    ALU.mult)
            v_atok = step("v", e_atok,
                          waits=[("t", t_hunt)] + ([("v", h2["v_xn"])] if h2 else []),
                          inc=[("v", 1)])
            t2["v_atok"] = v_atok

        def tail_thunks(ti):
            row0 = ti * P
            lsb = 12 * (ti % 2)
            eps_l = epsc[:, 0:1]
            fill = ls[:, lsb + 9:lsb + 10]
            x1t = x1bs[ti % 2]
            atok = tl2s[ti % 2]
            hc = HC[ti]
            hprev = HC.get(ti - 1)
            st = {}
            thunks = []

            w0 = [("v", hc["v_atok"])]
            if hprev and "v_xn" in hprev:
                w0 = w0 + [("v", hprev["v_xn"])]
            thunks += ln_stats_thunks(atok[:, :], ls[:, lsb + 8:lsb + 9],
                                      ls[:, lsb + 7:lsb + 8],
                                      ls[:, lsb + 2:lsb + 3], eps_l,
                                      lambda w0=w0: list(w0), lsb,
                                      tl1[:, :], st)

            def t_bb():
                def e_bb(a, lsb=lsb, atok=atok):
                    return a.activation(tl1[:, :], atok[:, :], AF.Identity,
                                        bias=ls[:, lsb + 2:lsb + 3],
                                        scale=ls[:, lsb + 7:lsb + 8])
                st["bb"] = step("a", e_bb, waits=[("a", st["b"])],
                                inc=[("a", 1)])

                def e_bb2(a, lsb=lsb):
                    a.activation(ls[:, lsb + 10:lsb + 11],
                                 epsc[:, 0:1], AF.Identity)
                    return a.activation(ls[:, lsb + 10:lsb + 11],
                                        epsc[:, 0:1], AF.Identity)
                st["bb2"] = step("a", e_bb2, inc=[("a", 1)])
            thunks.append(t_bb)

            def t_x1():
                def e_x1(v, x1t=x1t):
                    return v.tensor_add(x1t[:, :], x1t[:, :], tl1[:, :])
                st["x1"] = hc["v_x1"] = step("v", e_x1,
                                             waits=[("a", st["bb2"])],
                                             inc=[("v", 1)])
            thunks.append(t_x1)

            def t_store():
                step("gp", mk_dma(out_h[row0:row0 + P, :], x1t[:, :]),
                     waits=[("v", st["x1"])], inc=[("ds", 16)])
            thunks.append(t_store)

            st3 = {}
            thunks += ln_stats_thunks(x1t[:, :], m3s[:, ti:ti + 1],
                                      r3s[:, ti:ti + 1],
                                      ls[:, lsb + 2:lsb + 3], eps_l,
                                      lambda: [("v", st["x1"])], lsb,
                                      tl1[:, :], st3)

            def t_u():
                def e_u(a, ti=ti, lsb=lsb, atok=atok, x1t=x1t):
                    return a.activation(atok[:, :], x1t[:, :], AF.Identity,
                                        bias=ls[:, lsb + 2:lsb + 3],
                                        scale=r3s[:, ti:ti + 1])
                st["u"] = step("a", e_u, waits=[("a", st3["b"])],
                               inc=[("a", 1)])

                def e_u2(a, lsb=lsb):
                    a.activation(ls[:, lsb + 10:lsb + 11],
                                 epsc[:, 0:1], AF.Identity)
                    return a.activation(ls[:, lsb + 10:lsb + 11],
                                        epsc[:, 0:1], AF.Identity)
                st["u2"] = hc["a_u"] = step("a", e_u2, inc=[("a", 1)])
            thunks.append(t_u)

            st4 = {}
            thunks += ln_stats_thunks(atok[:, :], m4s[:, ti:ti + 1],
                                      r4s[:, ti:ti + 1],
                                      ls[:, lsb + 2:lsb + 3], epsc[:, 1:2],
                                      lambda: [("a", st["u2"])], lsb,
                                      tl1[:, :], st4, with_b=False)

            def t_xn():
                def e_xn(v, ti=ti, fill=fill, atok=atok):
                    v.tensor_scalar(tl1[:, :], atok[:, :], m4s[:, ti:ti + 1],
                                    r4s[:, ti:ti + 1], ALU.subtract, ALU.mult)
                    v.tensor_scalar(fill, epsc[:, 0:1], 1.0, None, ALU.mult)
                    v.tensor_mul(tl1[:, :], tl1[:, :], gam_rep[:, :])
                    v.tensor_scalar(fill, epsc[:, 0:1], 1.0, None, ALU.mult)
                    v.tensor_add(tl1[:, :], tl1[:, :], bet_rep[:, :])
                    v.tensor_scalar(fill, epsc[:, 0:1], 1.0, None, ALU.mult)
                    return v.tensor_reduce(gacc[:, ti:ti + 1], tl1[:, :],
                                           AX.X, ALU.max,
                                           apply_absolute_value=True)
                st["xn"] = hc["v_xn"] = step("v", e_xn,
                                             waits=[("a", st4["exp"])],
                                             inc=[("v", 1)])
            thunks.append(t_xn)
            return thunks, st

        PH = {}

        def pre_head_thunks(ti):
            row0 = ti * P
            lsb = 12 * (ti % 2)
            x_t = x1bs[ti % 2]
            st = {}
            PH[ti] = st
            thunks = []

            def t_load():
                h2 = HC.get(ti - 2)
                step("sync", mk_dma(x_t[:, :], x_h[row0:row0 + P, :]),
                     waits=([("a", h2["a_u"]), ("ds", 16 * (ti - 1))] if h2
                            else [("v", v_setup_done)]),
                     inc=[("dl", 16)])
                st["w_x"] = cnt["dl"]
            thunks.append(t_load)

            def waits0():
                w = [("dl", st["w_x"]), ("v", HC[ti - 1]["v_sk"])]
                h2 = HC.get(ti - 2)
                if h2 and "v_xn" in h2:
                    w.append(("v", h2["v_xn"]))
                return w
            lst = {}
            thunks += ln_stats_thunks(x_t[:, :], ls[:, lsb + 8:lsb + 9],
                                      ls[:, lsb + 7:lsb + 8],
                                      ls[:, lsb + 2:lsb + 3], epsc[:, 0:1],
                                      waits0, lsb, vtok[:, :], lst)

            def t_xin():
                def e_xin(a, lsb=lsb, x_t=x_t):
                    return a.activation(scr2[:, :], x_t[:, :], AF.Identity,
                                        bias=ls[:, lsb + 2:lsb + 3],
                                        scale=ls[:, lsb + 7:lsb + 8])
                st["xin"] = step(
                    "a", e_xin,
                    waits=[("a", lst["b"]), ("t", HC[ti - 1]["t_hk2"])],
                    inc=[("a", 1)])
            thunks.append(t_xin)
            return thunks

        prev_tail_v = v_setup_done
        for ti in range(nt):
            tl = tail_thunks(ti - 1)[0] if ti >= 1 else []
            if ti + 1 < nt:
                ph = pre_head_thunks(ti + 1)
                thunks = (tl[:14] + ph[:1] + tl[14:] + ph[1:]
                          if len(tl) >= 14
                          else tl + [lambda: None] * 12 + ph)
            else:
                thunks = tl
            it = iter(thunks)
            emit_head(ti, it)
            for th in it:
                th()
        thunks, stl = tail_thunks(nt - 1)
        for th in thunks:
            th()
        prev_tail_v = stl["xn"]

        def e_g1(v):
            return v.tensor_reduce(ls[:, 0:1], gacc[:, 0:nt], AX.X, ALU.max)
        v_g1 = step("v", e_g1, waits=[("v", prev_tail_v)], inc=[("v", 1)])

        def e_g2(t):
            return t.transpose(ps_3[0:1, 0:P], ls[:, 0:1], ids[:, :])
        t_g2 = step("t", e_g2, waits=[("v", v_g1)], inc=[("t", 1)])

        def e_g3a(v):
            return v.tensor_copy(srow[0:1, :], ps_3[0:1, 0:P])
        v_g3a = step("v", e_g3a, waits=[("t", t_g2)], inc=[("v", 1)])

        def e_g3(v):
            return v.tensor_reduce(s11[0:1, 2:3], srow[0:1, :], AX.X, ALU.max)
        v_g3 = step("v", e_g3, waits=[("v", v_g3a)], inc=[("v", 1)])

        def e_cc(gp):
            gp.dma_start(out=cc_in[0:1, 0:8], in_=s11[0:1, 0:8]).then_inc(
                sems["cc"], 16)
            gp.wait_ge(sems["cc"], 16)
            gp.collective_compute(
                "AllReduce", ALU.max,
                replica_groups=[list(range(8))],
                ins=[cc_in.ap()],
                outs=[cc_out.ap()],
            ).then_inc(sems["cb"], 1)
            gp.wait_ge(sems["cb"], 1)
            return gp.dma_start(out=srow[0:1, 0:8], in_=cc_out[0:1, 0:8])
        v_stores_a = 16 * nt
        step("gp", e_cc, waits=[("v", v_g3)], inc=[("cc", 16)])
        cc_done = cnt["cc"]

        def e_g5a(v):
            v.reciprocal(s11[0:1, 5:6], srow[0:1, 2:3])
            return v.tensor_copy(s11[0:1, 6:7], srow[0:1, 2:3])
        v_g5a = step("v", e_g5a, waits=[("cc", 32)], inc=[("v", 1)])

        def e_g5(v):
            return v.tensor_scalar(s11[0:1, 5:6], s11[0:1, 5:6], QB, None,
                                   ALU.mult)
        v_g5 = step("v", e_g5, waits=[("v", v_g5a)], inc=[("v", 1)])

        def e_g6(t):
            return t.matmul(ps_3[0:P, 0:2], ones_r[0:1, :], s11[0:1, 5:7],
                            start=True, stop=True)
        t_g6 = step("t", e_g6, waits=[("v", v_g5)], inc=[("t", 1)])

        def e_g7a(v):
            return v.tensor_copy(gsc[:, :], ps_3[0:P, 0:2])
        v_g7a = step("v", e_g7a, waits=[("t", t_g6)], inc=[("v", 1)])

        def e_g7b(v):
            v.tensor_scalar(gam_rep[:, :], gam_rep[:, :], gsc[:, 0:1], None,
                            ALU.mult)
            v.tensor_scalar(bet_rep[:, :], bet_rep[:, :], gsc[:, 0:1], None,
                            ALU.mult)
            v.tensor_scalar(fsc[:, :], gsc[:, 1:2], wbr[:, 0:1], 1.0 / QB,
                            ALU.mult, ALU.mult)
            v.tensor_mul(gacc[:, :], m3s[:, :], r3s[:, :])
            v.tensor_mul(rrs[:, :], r3s[:, :], r4s[:, :])
            v.tensor_add(bb2s[:, :], gacc[:, :], m4s[:, :])
            v.tensor_scalar(ls[:, 9:10], epsc[:, 0:1], 1.0, None, ALU.mult)
            v.tensor_mul(gacc[:, :], bb2s[:, :], r4s[:, :])
            v.tensor_scalar(ls[:, 9:10], epsc[:, 0:1], 1.0, None, ALU.mult)
            return v.tensor_scalar(bb2s[:, :], gacc[:, :], -1.0, None, ALU.mult)
        v_g7 = step("v", e_g7b, waits=[("v", v_g7a)], inc=[("v", 1)])

        def e_dbg(gp):
            gp.dma_start(out=dbg_h[0:1, 0:8], in_=s11[0:1, 0:8]).then_inc(
                sems["ds"], 16)
            return gp.dma_start(out=dbg_h[0:1, 8:16], in_=srow[0:1, 0:8])
        step("gp", e_dbg, waits=[("v", v_g7)], inc=[("ds", 16)])

        BH = {}

        def b_tail_thunks(ti):
            row0 = ti * P
            hb = BH[ti]
            st = {}
            thunks = []

            def t_yev():
                def e_yev(a, hb=hb):
                    return a.activation(x1bb[:, :], hb["psF"][:, :],
                                        AF.Identity, scale=fsc[:, 0:1])
                st["yev"] = hb["a_yev"] = step(
                    "a", e_yev, waits=[("t", hb["t_ffn"])], inc=[("a", 1)])

                def e_yev2(a):
                    a.activation(ls[:, 10:11], epsc[:, 0:1], AF.Identity)
                    return a.activation(ls[:, 10:11], epsc[:, 0:1],
                                        AF.Identity)
                st["yev2"] = step("a", e_yev2, inc=[("a", 1)])
            thunks.append(t_yev)

            st5 = {}
            thunks += ln_stats_thunks(x1bb[:, :], ls[:, 8:9], ls[:, 7:8],
                                      ls[:, 2:3], epsc[:, 0:1],
                                      lambda: [("a", st["yev2"])], 0,
                                      ablk[:, 0:1024], st5)

            def t_bf():
                def e_bf(a):
                    return a.activation(tl1[:, :], x1bb[:, :], AF.Identity,
                                        bias=ls[:, 2:3], scale=ls[:, 7:8])
                st["bf"] = step("a", e_bf, waits=[("a", st5["b"])],
                                inc=[("a", 1)])

                def e_bf2(a):
                    a.activation(ls[:, 10:11], epsc[:, 0:1], AF.Identity)
                    return a.activation(ls[:, 10:11], epsc[:, 0:1],
                                        AF.Identity)
                st["bf2"] = step("a", e_bf2, inc=[("a", 1)])
            thunks.append(t_bf)

            def t_x2():
                def e_x2(v, ti=ti):
                    return v.tensor_add(q0[:, :], xBs[ti % 2][:, :], tl1[:, :])
                st["x2"] = hb["v_x2"] = step(
                    "v", e_x2,
                    waits=[("a", st["bf2"]), ("ds", ds_b0 + 16 * ti)],
                    inc=[("v", 1)])
            thunks.append(t_x2)

            def t_store():
                step("gp", mk_dma(out_h[row0:row0 + P, :], q0[:, :]),
                     waits=[("v", st["x2"])], inc=[("ds", 16)])
            thunks.append(t_store)
            return thunks, st

        xBs = (x1ba, tl2a)
        ds_b0 = cnt["ds"]

        def b_pump(it, n=2):
            for _ in range(n):
                th = next(it, None)
                if th is None:
                    return
                th()

        def emit_up(ti):
            row0 = ti * P
            xB = xBs[ti % 2]
            me = BH.setdefault(ti, {})
            if ti < 2:
                lw = [("v", v_g7), ("ds", ds_b0)]
            else:
                lw = [("v", BH[ti - 2]["v_x2"]), ("a", BH[ti - 1]["a_up"])]
            step("sync", mk_dma(xB[:, :], out_h[row0:row0 + P, :]),
                 waits=lw, inc=[("dl", 16)])
            w_x1 = cnt["dl"]

            def e_dpre(a):
                a.activation(ls[:, 11:12], epsc[:, 0:1], AF.Identity)
                return a.activation(ls[:, 11:12], epsc[:, 0:1], AF.Identity)
            step("a", e_dpre, waits=[("dl", w_x1)], inc=[("a", 1)])

            def e_up(a, ti=ti, xB=xB):
                return a.activation(vtok[:, :], xB[:, :], AF.Identity,
                                    bias=bb2s[:, ti:ti + 1],
                                    scale=rrs[:, ti:ti + 1])
            me["a_up"] = step(
                "a", e_up,
                waits=[("v", BH[ti - 1]["v_z"])] if ti >= 1 else [],
                inc=[("a", 1)])

            def e_up2(a):
                a.activation(ls[:, 10:11], epsc[:, 0:1], AF.Identity)
                return a.activation(ls[:, 10:11], epsc[:, 0:1], AF.Identity)
            me["a_up2"] = step("a", e_up2, inc=[("a", 1)])

        emit_up(0)
        for ti in range(nt):
            hb1 = BH.get(ti - 1)
            me = BH[ti]
            a_up2 = me["a_up2"]
            tit = iter(b_tail_thunks(ti - 1)[0] if ti >= 1 else [])
            b_pump(tit)

            def e_z(v, ti=ti):
                v.tensor_mul(scr2[:, :], vtok[:, :], gam_rep[:, :])
                v.tensor_scalar(ls[:, 9:10], epsc[:, 0:1], 1.0, None, ALU.mult)
                return v.tensor_add(tl2b[:, :], scr2[:, :], bet_rep[:, :])
            v_z = step("v", e_z,
                       waits=[("a", a_up2)] +
                             ([("t", hb1["t_xqt"])] if hb1 else []),
                       inc=[("v", 1)])
            me["v_z"] = v_z
            b_pump(tit)

            def e_clip(v):
                return v.tensor_scalar(scr2[:, :], tl2b[:, :], CLIP_HI,
                                       CLIP_LO, ALU.min, ALU.max)
            v_clip = step("v", e_clip, waits=[("v", v_z)], inc=[("v", 1)])
            b_pump(tit)

            hb2 = BH.get(ti - 2)
            psT = (ps_2, ps_big[:, 1024:2048])[ti % 2]
            psF = (ps_3, ps_big[:, 0:1024])[ti % 2]
            xqh, xql = xqps[ti % 2]
            me["psF"] = psF

            def e_xqt(t, psT=psT):
                r = None
                for c in range(NC):
                    r = t.transpose(psT[:, c * P:(c + 1) * P],
                                    scr2[:, c * P:(c + 1) * P], ids[:, :])
                return r
            t_xqt = step("t", e_xqt,
                         waits=[("v", v_clip)] +
                               ([("v", hb2["v_lo"])] if hb2 else []),
                         inc=[("t", 1)])
            me["t_xqt"] = t_xqt
            b_pump(tit)

            def e_hi(a, xqh=xqh, psT=psT):
                return a.copy(xqh[:, :], psT[:, :])
            step("a", e_hi,
                 waits=[("t", t_xqt)] +
                       ([("t", hb2["t_ffn"])] if hb2 else []),
                 inc=[("a", 1)])

            def e_hi2(a):
                a.activation(ls[:, 10:11], epsc[:, 0:1], AF.Identity)
                return a.activation(ls[:, 10:11], epsc[:, 0:1], AF.Identity)
            a_hi = step("a", e_hi2, inc=[("a", 1)])

            def e_lo(v, xqh=xqh, xql=xql, psT=psT):
                return v.tensor_sub(xql[:, :], psT[:, :], xqh[:, :])
            v_lo = step("v", e_lo, waits=[("a", a_hi)], inc=[("v", 1)])
            me["v_lo"] = v_lo
            b_pump(tit)

            def e_ffn_hi(t, xqh=xqh, psF=psF):
                r = None
                for n in range(2):
                    dst = psF[:, n * 512:(n + 1) * 512]
                    for c in range(NC):
                        r = t.matmul(
                            dst, xqh[:, c * P:(c + 1) * P],
                            wf_s[:, c * D + n * 512:c * D + (n + 1) * 512],
                            start=(c == 0), stop=False)
                return r
            step("t", e_ffn_hi,
                 waits=[("a", a_hi)] +
                       ([("a", hb2["a_yev"])] if hb2 else []),
                 inc=[("t", 1)])

            def e_ffn_lo(t, xql=xql, psF=psF):
                r = None
                for n in range(2):
                    dst = psF[:, n * 512:(n + 1) * 512]
                    for c in range(NC):
                        r = t.matmul(
                            dst, xql[:, c * P:(c + 1) * P],
                            wf_s[:, c * D + n * 512:c * D + (n + 1) * 512],
                            start=False, stop=(c == NC - 1))
                return r
            t_ffn = step("t", e_ffn_lo, waits=[("v", v_lo)], inc=[("t", 1)])
            me["t_ffn"] = t_ffn
            for th in tit:
                th()
            if ti + 1 < nt:
                emit_up(ti + 1)

        for th in b_tail_thunks(nt - 1)[0]:
            th()

        def e_fin(gp):
            gp.wait_ge(sems["ds"], 16 * 2 * nt)
            return None
        step("gp", e_fin)

        def make_emitter(key):
            def run(eng):
                for e_key, waits, emit, incs in sched:
                    if e_key != key:
                        continue
                    for sk, val in waits:
                        eng.wait_ge(sems[sk], val)
                    inst = emit(eng)
                    for sk, amt in incs:
                        assert inst is not None, f"step on {key} has inc but returned None"
                        inst = inst.then_inc(sems[sk], amt)
            return run

        block.tensor(make_emitter("t"))
        block.vector(make_emitter("v"))
        block.scalar(make_emitter("a"))
        block.sync(make_emitter("sync"))
        block.gpsimd(make_emitter("gp"))

    return nc


_NC_CACHE = {}


def _get_nc(nt):
    if nt not in _NC_CACHE:
        _NC_CACHE[nt] = build_nc(nt)
    return _NC_CACHE[nt]


def kernel(L0_attn_input_seq_embs, L0_attn_W_Q, L0_attn_W_K, L0_attn_W_V,
           L0_ffn_weight, L0_ffn_ln_gamma, L0_ffn_ln_beta):
    x = np.ascontiguousarray(np.asarray(L0_attn_input_seq_embs, dtype=np.float32))
    B, T, Dm = x.shape
    assert Dm == D and T % P == 0
    nt = T // P
    wqT = np.ascontiguousarray(np.asarray(L0_attn_W_Q, dtype=np.float32).T)
    wkT = np.ascontiguousarray(np.asarray(L0_attn_W_K, dtype=np.float32).T)
    wvT = np.ascontiguousarray(np.asarray(L0_attn_W_V, dtype=np.float32).T)
    wf = np.ascontiguousarray(np.asarray(L0_ffn_weight, dtype=np.float32))
    lng = np.ascontiguousarray(np.asarray(L0_ffn_ln_gamma, dtype=np.float32)
                               .reshape(1, D))
    lnb = np.ascontiguousarray(np.asarray(L0_ffn_ln_beta, dtype=np.float32)
                               .reshape(1, D))
    ident = np.eye(P, dtype=np.float32)

    nc = _get_nc(nt)
    in_maps = [{"x": np.ascontiguousarray(x[i]), "wqT": wqT, "wkT": wkT,
                "wvT": wvT, "wf": wf, "lng": lng, "lnb": lnb, "ident": ident}
               for i in range(B)]
    res = run_bass_kernel_spmd(nc, in_maps, core_ids=list(range(B)))
    out = np.stack([np.asarray(r["out"]) for r in res.results], axis=0)
    return out.astype(np.float32)
